# version 11
# baseline (speedup 1.0000x reference)
import contextlib
import os as _os
import numpy as np
PBS = int(_os.environ.get("PBS", "0"))

import ml_dtypes
import concourse.bass as bass
import concourse.mybir as mybir
from concourse.bass_utils import run_bass_kernel_spmd

F32 = mybir.dt.float32
BF16 = mybir.dt.bfloat16
AF = mybir.ActivationFunctionType
ALU = mybir.AluOpType
BF = ml_dtypes.bfloat16

NCORE = 8
D = 2048
TCORE = 2048
TILE = 512
NT = TCORE // TILE
NCH = D // 128
H, HKV, HD, E = 16, 8, 128, 64
G = 16
NG = TILE // G
NGH = NG // 2
SEQ = 8192
THETA = 10000.0
MAGIC = float(1.5 * 2 ** 23)
INV2PI = float(1.0 / (2 * np.pi))
TWOPI = float(2 * np.pi)

ENGS = ("pe", "act", "dve", "pool", "sp")


class Sem:
    def __init__(self, es, nc, name):
        self.h = es.enter_context(nc.semaphore(name))
        self.n = 0


class Prog:

    def __init__(self):
        self.streams = {e: [] for e in ENGS}

    def op(self, eng, builder, inc=None, inc_k=1):
        if inc is not None:
            inc.n += inc_k

            def f(e, b=builder, h=inc.h, k=inc_k):
                b(e).then_inc(h, k)
        else:
            def f(e, b=builder):
                b(e)
        self.streams[eng].append(f)

    def wait(self, eng, sem, val=None):
        v = sem.n if val is None else val
        if v <= 0:
            return
        self.streams[eng].append(lambda e, h=sem.h, v=v: e.wait_ge(h, v))

    def dma(self, eng, out_ap, in_ap, sem):
        sem.n += 16
        self.streams[eng].append(
            lambda e, o=out_ap, i=in_ap, h=sem.h: e.dma_start(out=o, in_=i).then_inc(h, 16))


def build_program(debug=False):
    nc = bass.Bass("TRN2", target_bir_lowering=False)

    def inp(name, shape, dt=F32):
        return nc.dram_tensor(name, shape, dt, kind="ExternalInput").ap()

    x_d = inp("x", [D, TCORE])
    x2_d = inp("x2", [D, TCORE])
    wq_d = inp("wqT", [D, D])
    wk_d = inp("wkT", [D, HKV * HD])
    wv_d = inp("wvT", [D, HKV * HD])
    wo_d = inp("woT", [D, D])
    wf_d = inp("wfT", [D, D])
    rr_d = inp("rramT", [D, D])
    rm_d = inp("rmT", [128, E])
    bq_d = inp("bqc", [128, 16])
    bk_d = inp("bkc", [128, 8])
    bv_d = inp("bvc", [128, 8])
    bo_d = inp("boc", [128, 16])
    cos_d = inp("cosT", [128, TCORE], BF16)
    sin_d = inp("sinS", [128, TCORE], BF16)
    id_d = inp("ident", [128, 128], BF16)
    mk_d = inp("mask2", [128, 256], BF16)
    oh_d = inp("oneh", [128, 24 * 24], BF16)
    or_d = inp("onesrow", [1, 128], BF16)
    oc_d = inp("onescol", [128, 1], BF16)
    o2_d = inp("ones2d", [128, 128], BF16)
    cc_d = inp("ccol2", [128, 1])
    ph_d = inp("phcol", [128, 1])
    ep_d = inp("epscol", [128, 1])

    out_d = nc.dram_tensor("out", [D, TCORE], F32, kind="ExternalOutput").ap()
    if debug:
        dbg_xn = nc.dram_tensor("dbg_xn", [128, NCH * TILE], BF16, kind="ExternalOutput").ap()
        dbg_pkq = nc.dram_tensor("dbg_pkq", [128, H * TILE], BF16, kind="ExternalOutput").ap()
        dbg_pkk = nc.dram_tensor("dbg_pkk", [128, HKV * TILE], BF16, kind="ExternalOutput").ap()
        dbg_qp = nc.dram_tensor("dbg_qp", [128, H * TILE], BF16, kind="ExternalOutput").ap()
        dbg_kp = nc.dram_tensor("dbg_kp", [128, HKV * TILE], BF16, kind="ExternalOutput").ap()
        dbg_vf = nc.dram_tensor("dbg_vf", [128, HKV * TILE], BF16, kind="ExternalOutput").ap()
        dbg_inv = nc.dram_tensor("dbg_inv", [24, 512], F32, kind="ExternalOutput").ap()
        dbg_gm = nc.dram_tensor("dbg_gm", [128, 512], BF16, kind="ExternalOutput").ap()
        dbg_u2 = nc.dram_tensor("dbg_u2", [128, NCH * TILE], BF16, kind="ExternalOutput").ap()
        dbg_og = nc.dram_tensor("dbg_og", [128, H * TILE], BF16, kind="ExternalOutput").ap()
        dbg_th = nc.dram_tensor("dbg_th", [128, 1024], F32, kind="ExternalOutput").ap()
        dbg_u2b = nc.dram_tensor("dbg_u2b", [128, NCH * TILE], BF16, kind="ExternalOutput").ap()
        dbg_thb = nc.dram_tensor("dbg_thb", [128, 1024], F32, kind="ExternalOutput").ap()
        dbg_og2 = nc.dram_tensor("dbg_og2", [128, H * TILE], BF16, kind="ExternalOutput").ap()
        dbg_st = nc.dram_tensor("dbg_st", [128, 1024], F32, kind="ExternalOutput").ap()
        dbg_g = nc.dram_tensor("dbg_g", [128, NCH * TILE], F32, kind="ExternalOutput").ap()
        dbg_sg = nc.dram_tensor("dbg_sg", [128, NCH * TILE], F32, kind="ExternalOutput").ap()
        dbg_rows = nc.dram_tensor("dbg_rows", [1, 1024], F32, kind="ExternalOutput").ap()
        dbg_rows2 = nc.dram_tensor("dbg_rows2", [1, 1024], F32, kind="ExternalOutput").ap()

    scratch_kind = "ExternalOutput"
    o_cont = nc.dram_tensor("o_cont", [128, H * TCORE], BF16, kind=scratch_kind).ap()
    wk_scr = nc.dram_tensor("wk_scr", [NCH, 128, HKV * HD], BF16, kind="Internal").ap()
    wv_scr = nc.dram_tensor("wv_scr", [NCH, 128, HKV * HD], BF16, kind="Internal").ap()
    wf_scr = nc.dram_tensor("wf_scr", [NCH, 128, D], BF16, kind="Internal").ap()

    with contextlib.ExitStack() as es:
        A = es.enter_context

        wq_sb = A(nc.sbuf_tensor("wq_sb", [128, NCH * D], BF16))
        wkv_sb = A(nc.sbuf_tensor("wkv_sb", [128, NCH * 1024], BF16))
        wf2_sb = A(nc.sbuf_tensor("wf2_sb", [128, 8 * D], BF16))
        pkk_v = wf2_sb[:, 0:HKV * TILE]
        kp_v = wf2_sb[:, HKV * TILE:2 * HKV * TILE]
        vf_v = wf2_sb[:, 2 * HKV * TILE:3 * HKV * TILE]
        ohh_v = wf2_sb[:, 3 * HKV * TILE:4 * HKV * TILE]
        xn_sb = A(nc.sbuf_tensor("xn_sb", [128, NCH * TILE], BF16))
        pkq_sb = A(nc.sbuf_tensor("pkq_sb", [128, H * TILE], BF16))
        qp_sb = A(nc.sbuf_tensor("qp_sb", [128, H * TILE], BF16))
        scr = A(nc.sbuf_tensor("scr", [128, 4096], F32))
        lnM = scr[:, 0:512]
        lnT1 = scr[:, 512:1024]
        lnT2 = scr[:, 1024:1536]
        lnH = scr[:, 1536:2048]
        lnL = scr[:, 2048:2560]
        fsl = scr[:, 2560:3072]
        xsl = scr[:, 3072:4096]
        srow = lnH[0:1, :]
        sqrow = lnL[0:1, :]
        invrow = lnL[0:1, :]
        nsq_v = scr[0:24, 2560:3072]
        pbuf = scr[:, 3072:3584]
        kbuf = scr[:, 3584:4096]
        cosr = A(nc.sbuf_tensor("cosr", [128, 512], BF16))
        sinr = A(nc.sbuf_tensor("sinr", [128, 512], BF16))
        xb16 = A(nc.sbuf_tensor("xb16", [128, 1024], BF16))
        qa_b = A(nc.sbuf_tensor("qa_b", [128, 1024], BF16))
        qm_b = A(nc.sbuf_tensor("qm_b", [128, 1024], BF16))
        inv_sb = A(nc.sbuf_tensor("inv_sb", [24, 512], F32))
        gm_sb = A(nc.sbuf_tensor("gm_sb", [128, 512], BF16))
        v8_sb = A(nc.sbuf_tensor("v8_sb", [128, 256], BF16))
        rt_sb = A(nc.sbuf_tensor("rt_sb", [128, E], BF16))
        id_sb = A(nc.sbuf_tensor("id_sb", [128, 128], BF16))
        mk_sb = A(nc.sbuf_tensor("mk_sb", [128, 256], BF16))
        ohot = A(nc.sbuf_tensor("ohot", [128, 24 * 24], BF16))
        orow = A(nc.sbuf_tensor("orow", [1, 128], BF16))
        ocol = A(nc.sbuf_tensor("ocol", [128, 1], BF16))
        o2_sb = A(nc.sbuf_tensor("o2_sb", [128, 128], BF16))
        bq_sb = A(nc.sbuf_tensor("bq_sb", [128, 16], F32))
        bk_sb = A(nc.sbuf_tensor("bk_sb", [128, 8], F32))
        bv_sb = A(nc.sbuf_tensor("bv_sb", [128, 8], F32))
        bo_sb = A(nc.sbuf_tensor("bo_sb", [128, 16], F32))
        cc_sb = A(nc.sbuf_tensor("cc_sb", [128, 1], F32))
        ph_sb = A(nc.sbuf_tensor("ph_sb", [128, 1], F32))
        ep_sb = A(nc.sbuf_tensor("ep_sb", [128, 1], F32))
        flshb = A(nc.sbuf_tensor("flshb", [1, 128], BF16))
        flshf = A(nc.sbuf_tensor("flshf", [1, 64], F32))
        flshc = A(nc.sbuf_tensor("flshc", [128, 1], F32))


        pb = [A(nc.psum_tensor(f"pb{i}", [128, 512], F32)) for i in range(7)]
        pvt = A(nc.psum_tensor("pvt", [128, 512], BF16))
        pfl = pvt[0:1, 256:512].bitcast(F32)

        s_ld = Sem(es, nc, "s_ld")
        s_x = Sem(es, nc, "s_x")
        s_w = Sem(es, nc, "s_w")
        s_ws = Sem(es, nc, "s_ws")
        s_kv = Sem(es, nc, "s_kv")
        s_rot = Sem(es, nc, "s_rot")
        s_ir = Sem(es, nc, "s_ir")
        s_u = Sem(es, nc, "s_u")
        s_os = Sem(es, nc, "s_os")
        s_cs = Sem(es, nc, "s_cs")
        s_out = Sem(es, nc, "s_out")
        p_pe = Sem(es, nc, "p_pe")
        p_act = Sem(es, nc, "p_act")
        p_dve = Sem(es, nc, "p_dve")
        p_pool = Sem(es, nc, "p_pool")

        P = Prog()

        def fence(eng, src_ap, inc):
            n = src_ap.free_size()
            if eng == "act":
                P.op("act", lambda e, s=src_ap, n=n: e.copy(out=flshf[0:1, 0:n], in_=s), inc=inc)
            else:
                P.op(eng, lambda e, s=src_ap, n=n: e.tensor_copy(out=flshf[0:1, 0:n], in_=s), inc=inc)

        def dma_flush(dst_col_ap, sem, eng="sp"):
            P.dma(eng, flshc[:], dst_col_ap, sem)

        def pe_flush():
            P.op("pe", lambda e: e.matmul(out=pfl, lhsT=ocol[:], rhs=o2_sb[:],
                                          start=True, stop=True))
            P.op("pe", lambda e: e.matmul(out=pfl, lhsT=ocol[:], rhs=o2_sb[:],
                                          start=True, stop=True), inc=p_pe)

        def join(waiters=ENGS, producers=()):
            for w in waiters:
                for s in producers:
                    P.wait(w, s)

        for dst, src_ in [(id_sb[:], id_d), (mk_sb[:], mk_d),
                          (ohot[:], oh_d), (orow[:], or_d), (ocol[:], oc_d),
                          (bq_sb[:], bq_d), (bk_sb[:], bk_d), (bv_sb[:], bv_d),
                          (bo_sb[:], bo_d), (cc_sb[:], cc_d), (ph_sb[:], ph_d),
                          (ep_sb[:], ep_d), (o2_sb[:], o2_d)]:
            P.dma("sp", dst, src_, s_ld)
        P.dma("sp", pbuf[:, 0:E], rm_d, s_ld)
        P.wait("dve", s_ld)
        P.op("dve", lambda e: e.tensor_copy(out=rt_sb[:], in_=pbuf[:, 0:E]), inc=p_dve)

        def cast_stream(dst_of_piece, src_ap, ncols):
            for pc in range(ncols // 512):
                slot = pbuf if pc % 2 == 0 else kbuf
                P.wait("sp", p_dve)
                P.dma("sp", slot, src_ap[:, pc * 512:(pc + 1) * 512], s_w)
                P.wait("dve", s_w)
                P.op("dve", lambda e, slot=slot, pc=pc: e.tensor_copy(
                    out=dst_of_piece(pc), in_=slot), inc=p_dve)

        for c in range(NCH):
            cast_stream(lambda pc, c=c: wq_sb[:, c * D + pc * 512:c * D + (pc + 1) * 512],
                        wq_d[c * 128:(c + 1) * 128, :], D)
        for c in range(NCH):
            P.wait("dve", s_ws)
            cast_stream(lambda pc: qa_b[:, pc * 512:(pc + 1) * 512],
                        wk_d[c * 128:(c + 1) * 128, :], 1024)
            cast_stream(lambda pc: qm_b[:, pc * 512:(pc + 1) * 512],
                        wv_d[c * 128:(c + 1) * 128, :], 1024)
            P.wait("sp", p_dve)
            P.dma("sp", wk_scr[c], qa_b[:], s_ws)
            P.dma("sp", wv_scr[c], qm_b[:], s_ws)
        for c in range(NCH):
            for hh in range(2):
                P.wait("pool", s_ws)
                for pc in range(2):
                    col0 = hh * 1024 + pc * 512
                    P.wait("sp", p_pool)
                    P.wait("sp", p_dve)
                    P.dma("sp", pbuf, wf_d[c * 128:(c + 1) * 128, col0:col0 + 512], s_w)
                    P.dma("sp", kbuf, rr_d[c * 128:(c + 1) * 128, col0:col0 + 512], s_w)
                    P.wait("dve", s_w)
                    P.op("dve", lambda e: e.tensor_scalar(out=lnT2, in0=kbuf, scalar1=0.5, scalar2=None, op0=ALU.mult))
                    P.op("dve", lambda e: e.tensor_single_scalar(out=lnM, in_=pbuf, scalar=0.2, op=ALU.is_gt))
                    P.op("dve", lambda e: e.tensor_single_scalar(out=lnT1, in_=pbuf, scalar=-0.2, op=ALU.is_lt))
                    P.op("dve", lambda e: e.tensor_sub(out=lnM, in0=lnM, in1=lnT1))
                    P.op("dve", lambda e, pc=pc: e.scalar_tensor_tensor(
                        out=qa_b[:, pc * 512:(pc + 1) * 512], in0=lnM, scalar=0.3, in1=lnT2,
                        op0=ALU.mult, op1=ALU.add), inc=p_dve)
                P.wait("sp", p_dve)
                P.dma("sp", wf_scr[c][:, hh * 1024:(hh + 1) * 1024], qa_b[:], s_ws)
        join(ENGS, [p_dve, p_pool, s_ws, s_ld])

        for j in range(NT):
            t0 = j * TILE
            P.wait("sp", p_pool)
            P.dma("sp", cosr[:], cos_d[:, t0:t0 + 512], s_cs)
            P.dma("sp", sinr[:], sin_d[:, t0:t0 + 512], s_cs)
            P.dma("sp", flshc[:].bitcast(BF16)[:, 0:1], sinr[:, 511:512], s_cs)
            P.dma("sp", flshc[:].bitcast(BF16)[:, 0:1], cosr[:, 511:512], s_cs)
            P.wait("pool", p_pe)
            for c in range(NCH):
                P.dma("pool", wkv_sb[:, c * 1024:(c + 1) * 1024], wk_scr[c], s_kv)
                dma_flush(wkv_sb[:, c * 1024 + 1022:c * 1024 + 1024].bitcast(F32), s_kv, eng="pool")

            pe0, pl0 = p_pe.n, p_pool.n
            for c in range(NCH):
                sl = c % 2
                slot = pbuf if sl == 0 else kbuf
                if c >= 2:
                    P.wait("sp", p_pe, pe0 + (c - 1))
                    P.wait("sp", p_pool, pl0 + 2 * (c - 1))
                P.dma("sp", slot, x_d[c * 128:(c + 1) * 128, t0:t0 + 512], s_x)
                dma_flush(slot[:, 511:512], s_x)
                P.wait("dve", s_x)
                P.op("dve", lambda e, slot=slot, sl=sl: e.tensor_copy(
                    out=xb16[:, sl * 512:(sl + 1) * 512], in_=slot), inc=p_dve)
                fence("dve", xb16[0:1, sl * 512:sl * 512 + 64].bitcast(F32)[0:1, 0:32], p_dve)
                P.wait("pool", p_dve)
                P.op("pool", lambda e, sl=sl: e.tensor_mul(
                    out=qa_b[:, sl * 512:(sl + 1) * 512], in0=xb16[:, sl * 512:(sl + 1) * 512],
                    in1=xb16[:, sl * 512:(sl + 1) * 512]), inc=p_pool)
                fence("pool", qa_b[0:1, sl * 512:sl * 512 + 64].bitcast(F32)[0:1, 0:32], p_pool)
                P.wait("pe", p_pool)
                P.op("pe", lambda e, c=c, sl=sl: e.matmul(
                    out=pb[2][:], lhsT=o2_sb[:], rhs=xb16[:, sl * 512:(sl + 1) * 512],
                    start=(c == 0), stop=(c == NCH - 1)))
                P.op("pe", lambda e, c=c, sl=sl: e.matmul(
                    out=pb[3][:], lhsT=o2_sb[:], rhs=qa_b[:, sl * 512:(sl + 1) * 512],
                    start=(c == 0), stop=(c == NCH - 1)), inc=p_pe)
            pe_flush()
            P.wait("dve", p_pe)
            P.op("dve", lambda e: e.tensor_scalar(out=lnM, in0=pb[2][:], scalar1=1.0 / D, scalar2=None, op0=ALU.mult))
            P.op("dve", lambda e: e.tensor_mul(out=lnT1, in0=lnM, in1=lnM))
            P.op("dve", lambda e: e.scalar_tensor_tensor(
                out=lnT2, in0=pb[3][:], scalar=1.0 / D, in1=lnT1, op0=ALU.mult, op1=ALU.subtract))
            fence("dve", lnT2[0:1, 0:64], p_dve)
            P.wait("act", p_dve)
            P.op("act", lambda e: e.activation(out=lnT1, in_=lnT2, func=AF.Sqrt, bias=ep_sb.ap()))
            fence("act", lnT1[0:1, 0:64], p_act)
            P.wait("dve", p_act)
            P.op("dve", lambda e: e.reciprocal(out=lnT2, in_=lnT1))
            fence("dve", lnT2[0:1, 0:64], p_dve)
            join(("pool", "sp", "act"), [p_dve])
            if debug and j == 0:
                P.dma("sp", dbg_st[:, 0:512], lnM, s_ld)
                P.dma("sp", dbg_st[:, 512:1024], lnT2, s_ld)
                P.dma("sp", dbg_rows[0:1, 0:512], lnM[0:1, :], s_ld)
                P.dma("sp", dbg_rows[0:1, 512:1024], lnT2[0:1, :], s_ld)
                P.op("act", lambda e: e.copy(out=pbuf[0:1, :], in_=pb[0][0:1, :]))
                P.op("act", lambda e: e.copy(out=kbuf[0:1, :], in_=pb[1][0:1, :]))
                fence("act", kbuf[0:1, 0:64], p_act)
                P.wait("sp", p_act)
                P.dma("sp", dbg_rows2[0:1, 0:512], pbuf[0:1, :], s_ld)
                P.dma("sp", dbg_rows2[0:1, 512:1024], kbuf[0:1, :], s_ld)
                join(ENGS, [s_ld])

            for c in range(NCH):
                sl = c % 2
                slot = pbuf if sl == 0 else kbuf
                if c >= 1:
                    P.wait("act", p_pool)
                P.dma("act", slot, x_d[c * 128:(c + 1) * 128, t0:t0 + 512], s_x)
                dma_flush(slot[:, 511:512], s_x, eng="act")
                P.wait("pool", s_x)
                P.op("pool", lambda e, slot=slot: e.tensor_sub(out=fsl, in0=slot, in1=lnM))
                P.op("pool", lambda e, c=c: e.tensor_mul(
                    out=xn_sb[:, c * 512:(c + 1) * 512], in0=fsl, in1=lnT2), inc=p_pool)
            join(ENGS, [p_pool])

            planes = [("q", i) for i in range(H)] + [("k", i) for i in range(HKV)] + [("v", i) for i in range(HKV)]
            evac_mark = []
            wv_loaded = False
            for idx, (kind, pi) in enumerate(planes):
                bank = pb[3 + (idx % 4)]
                if kind == "v" and not wv_loaded:
                    P.wait("pool", p_pe)
                    for c in range(NCH):
                        P.dma("pool", wkv_sb[:, c * 1024:(c + 1) * 1024], wv_scr[c], s_kv)
                        dma_flush(wkv_sb[:, c * 1024 + 1022:c * 1024 + 1024].bitcast(F32), s_kv, eng="pool")
                    wv_loaded = True
                if idx >= 4:
                    P.wait("pe", p_act, evac_mark[idx - 4])
                if kind == "q":
                    for c in range(NCH):
                        P.op("pe", lambda e, c=c, pi=pi, bank=bank: e.matmul(
                            out=bank[:], lhsT=wq_sb[:, c * D + pi * 128:c * D + (pi + 1) * 128],
                            rhs=xn_sb[:, c * 512:(c + 1) * 512], start=(c == 0), stop=(c == NCH - 1)),
                            inc=(p_pe if c == NCH - 1 else None))
                    pe_flush()
                else:
                    P.wait("pe", s_kv)
                    for c in range(NCH):
                        P.op("pe", lambda e, c=c, pi=pi, bank=bank: e.matmul(
                            out=bank[:], lhsT=wkv_sb[:, c * 1024 + pi * 128:c * 1024 + (pi + 1) * 128],
                            rhs=xn_sb[:, c * 512:(c + 1) * 512], start=(c == 0), stop=(c == NCH - 1)),
                            inc=(p_pe if c == NCH - 1 else None))
                    pe_flush()
                P.wait("act", p_pe)
                if kind == "v":
                    P.op("act", lambda e, pi=pi, bank=bank: e.activation(
                        out=vf_v.rearrange("p (g c) -> p g c", g=NG)[:, :, pi * G:(pi + 1) * G],
                        in_=bank[:], func=AF.Identity, bias=bv_sb[:, pi:pi + 1]), inc=p_act)
                    evac_mark.append(p_act.n)
                else:
                    half = pi % 2
                    bias = bq_sb[:, pi:pi + 1] if kind == "q" else bk_sb[:, pi:pi + 1]
                    if half == 0:
                        P.wait("act", p_pool)
                        P.wait("act", p_pe)
                    P.op("act", lambda e, half=half, bank=bank, bias=bias: e.activation(
                        out=qa_b[:, half * 512:(half + 1) * 512], in_=bank[:],
                        func=AF.Identity, bias=bias), inc=p_act)
                    evac_mark.append(p_act.n)
                    if half == 1:
                        park = pkq_sb[:] if kind == "q" else pkk_v
                        pcols = (pi - 1) * 512
                        P.wait("sp", p_act)
                        P.wait("sp", p_pool)
                        P.dma("sp", qm_b[0:64, :], qa_b[64:128, :], s_rot)
                        P.dma("sp", qm_b[64:128, :], qa_b[0:64, :], s_rot)
                        P.dma("sp", flshb[0:1, 0:64], qm_b[0:1, 960:1024], s_rot)
                        P.dma("sp", flshb[0:1, 64:128], qm_b[64:65, 960:1024], s_rot)
                        P.wait("pool", s_rot)
                        P.wait("pool", s_cs)
                        P.wait("pool", p_dve)
                        for hf in range(2):
                            P.op("pool", lambda e, hf=hf: e.tensor_mul(
                                out=xb16[:, hf * 512:(hf + 1) * 512],
                                in0=qa_b[:, hf * 512:(hf + 1) * 512], in1=cosr[:]))
                            P.op("pool", lambda e, hf=hf: e.tensor_mul(
                                out=qm_b[:, hf * 512:(hf + 1) * 512],
                                in0=qm_b[:, hf * 512:(hf + 1) * 512], in1=sinr[:]),
                                inc=(p_pool if hf == 1 else None))
                        P.wait("dve", p_pool)
                        P.op("dve", lambda e, park=park, pcols=pcols: e.tensor_add(
                            out=park[:, pcols:pcols + 1024], in0=xb16[:], in1=qm_b[:]), inc=p_dve)
                        fence("dve", park[0:1, pcols:pcols + 64].bitcast(F32)[0:1, 0:32], p_dve)
                        P.wait("act", p_dve)
                        P.op("act", lambda e, park=park, pcols=pcols: e.activation(
                            out=qa_b[:], in_=park[:, pcols:pcols + 1024], func=AF.Square), inc=p_act)
                        P.wait("pe", p_act)
                        jpl = (pi - 1) + (0 if kind == "q" else H)
                        for dpl in range(2):
                            jj = jpl + dpl
                            P.op("pe", lambda e, jj=jj, dpl=dpl: e.matmul(
                                out=pb[2][0:24, :], lhsT=ohot[:, jj * 24:(jj + 1) * 24],
                                rhs=qa_b[:, dpl * 512:(dpl + 1) * 512],
                                start=(jj == 0), stop=(jj == 23)), inc=p_pe)
            pe_flush()
            fence("act", vf_v[0:1, 0:64].bitcast(F32)[0:1, 0:32], p_act)
            join(ENGS, [p_pe, p_act, p_dve, p_pool])

            P.op("act", lambda e: e.copy(out=nsq_v, in_=pb[2][0:24, :]))
            fence("act", nsq_v[0:1, 0:64], p_act)
            P.wait("dve", p_act)
            P.op("dve", lambda e: e.tensor_scalar_max(out=nsq_v, in0=nsq_v, scalar1=1e-12))
            fence("dve", nsq_v[0:1, 0:64], p_dve)
            P.wait("act", p_dve)
            P.op("act", lambda e: e.activation(out=inv_sb[:], in_=nsq_v, func=AF.Sqrt))
            fence("act", inv_sb[0:1, 0:64], p_act)
            P.wait("dve", p_act)
            P.op("dve", lambda e: e.reciprocal(out=inv_sb[:], in_=inv_sb[:]))
            P.op("dve", lambda e: e.tensor_copy(out=qa_b[0:24, 0:512], in_=inv_sb[:]))
            P.op("dve", lambda e: e.tensor_sub(out=nsq_v, in0=inv_sb[:], in1=qa_b[0:24, 0:512]))
            P.op("dve", lambda e: e.tensor_copy(out=qa_b[0:24, 512:1024], in_=nsq_v))
            fence("dve", qa_b[0:24, 512:576].bitcast(F32)[0:1, 0:32], p_dve)
            join(("sp", "pe", "act", "pool"), [p_dve])

            if debug and j == 0:
                P.wait("sp", p_dve)
                P.dma("sp", dbg_xn, xn_sb[:], s_ld)
                P.dma("sp", dbg_pkq, pkq_sb[:], s_ld)
                P.dma("sp", dbg_pkk, pkk_v, s_ld)
                P.dma("sp", dbg_vf, vf_v, s_ld)
                P.dma("sp", dbg_inv, inv_sb[:], s_ld)

            for jpl in range(24):
                kind = "q" if jpl < H else "k"
                pi = jpl if jpl < H else jpl - H
                park = pkq_sb[:] if kind == "q" else pkk_v
                feats = qp_sb[:] if kind == "q" else kp_v
                P.wait("sp", p_pe)
                P.dma("sp", qm_b[0:1, 0:512], qa_b[jpl:jpl + 1, 0:512], s_ir)
                P.dma("sp", qm_b[0:1, 512:1024], qa_b[jpl:jpl + 1, 512:1024], s_ir)
                P.dma("sp", flshb[0:1, 0:64], qm_b[0:1, 448:512], s_ir)
                P.dma("sp", flshb[0:1, 64:128], qm_b[0:1, 960:1024], s_ir)
                P.wait("pe", s_ir)
                P.wait("pe", p_act)
                P.wait("pe", p_dve)
                P.op("pe", lambda e: e.matmul(out=pb[5][:], lhsT=orow[:], rhs=qm_b[0:1, 0:512], start=True, stop=False))
                P.op("pe", lambda e: e.matmul(out=pb[5][:], lhsT=orow[:], rhs=qm_b[0:1, 512:1024], start=False, stop=True), inc=p_pe)
                P.op("pe", lambda e, park=park, pi=pi: e.matmul(
                    out=pb[6][0:64, :], lhsT=rt_sb[:], rhs=park[:, pi * 512:(pi + 1) * 512], start=True, stop=True))
                P.op("pe", lambda e, park=park, pi=pi: e.matmul(
                    out=pb[6][64:128, :], lhsT=rt_sb[:], rhs=park[:, pi * 512:(pi + 1) * 512], start=True, stop=True), inc=p_pe)
                pe_flush()
                P.wait("act", p_pe)
                P.wait("act", p_dve)
                P.op("act", lambda e: e.copy(out=fsl, in_=pb[5][:]), inc=p_act)
                P.wait("dve", p_pe)
                P.wait("dve", p_act)
                P.wait("dve", p_pool)
                P.op("dve", lambda e: e.tensor_mul(out=pbuf, in0=pb[6][:], in1=fsl))
                P.op("dve", lambda e: e.tensor_scalar(
                    out=kbuf, in0=pbuf, scalar1=INV2PI, scalar2=cc_sb[:, 0:1], op0=ALU.mult, op1=ALU.add))
                P.op("dve", lambda e: e.tensor_scalar(
                    out=kbuf, in0=kbuf, scalar1=MAGIC, scalar2=-TWOPI, op0=ALU.subtract, op1=ALU.mult), inc=p_dve)
                fence("dve", kbuf[0:1, 0:64], p_dve)
                P.wait("pool", p_dve)
                P.wait("pool", p_act)
                P.op("pool", lambda e: e.tensor_add(out=lnM, in0=kbuf, in1=pbuf), inc=p_pool)
                fence("pool", lnM[0:1, 0:64], p_pool)
                P.wait("act", p_pool)
                P.op("act", lambda e, feats=feats, pi=pi: e.activation(
                    out=feats.rearrange("p (g c) -> p g c", g=NG)[:, :, pi * G:(pi + 1) * G],
                    in_=lnM, func=AF.Sin, bias=ph_sb.ap()), inc=p_act)
            fence("act", qp_sb[0:1, 0:64].bitcast(F32)[0:1, 0:32], p_act)
            join(ENGS, [p_act, p_pe, p_dve, p_pool])

            if debug and j == 0:
                P.wait("sp", p_act)
                P.dma("sp", dbg_qp, qp_sb[:], s_ld)
                P.dma("sp", dbg_kp, kp_v, s_ld)

            for half in range(2):
                for gg in range(NGH // 2):
                    g0 = half * NGH + gg * 2
                    P.wait("pe", p_dve)
                    P.wait("pe", p_act)
                    for dg in range(2):
                        g = g0 + dg
                        P.op("pe", lambda e, g=g, dg=dg: e.matmul(
                            out=pb[5][:, dg * 256:(dg + 1) * 256], lhsT=kp_v[:, g * 128:(g + 1) * 128],
                            rhs=qp_sb[:, g * 256:(g + 1) * 256], start=True, stop=True))
                        P.op("pe", lambda e, g=g, dg=dg: e.transpose(
                            out=pvt[:, dg * 128:(dg + 1) * 128], in_=vf_v[:, g * 128:(g + 1) * 128],
                            identity=id_sb[:]), inc=(p_pe if dg == 1 else None))
                    pe_flush()
                    P.wait("dve", p_pe)
                    for dg in range(2):
                        P.op("dve", lambda e, dg=dg: e.tensor_mul(
                            out=gm_sb[:, dg * 256:(dg + 1) * 256],
                            in0=pb[5][:, dg * 256:(dg + 1) * 256], in1=mk_sb[:]),
                            inc=(p_dve if dg == 1 else None))
                    fence("dve", gm_sb[0:1, 0:64].bitcast(F32)[0:1, 0:32], p_dve)
                    P.wait("act", p_pe)
                    P.op("act", lambda e: e.copy(out=v8_sb[:], in_=pvt[:, 0:256]))
                    fence("act", v8_sb[0:1, 0:64].bitcast(F32)[0:1, 0:32], p_act)
                    P.wait("pe", p_dve)
                    P.wait("pe", p_act)
                    for dg in range(2):
                        P.op("pe", lambda e, dg=dg: e.matmul(
                            out=pb[6][:, dg * 256:(dg + 1) * 256], lhsT=v8_sb[:, dg * 128:(dg + 1) * 128],
                            rhs=gm_sb[:, dg * 256:(dg + 1) * 256], start=True, stop=True),
                            inc=(p_pe if dg == 1 else None))
                    pe_flush()
                    P.wait("act", p_pe)
                    for dg in range(2):
                        gl = (g0 + dg) % NGH
                        P.op("act", lambda e, dg=dg, gl=gl: e.copy(
                            out=ohh_v.rearrange("p (h t) -> p h t", h=H)[:, :, gl * G:(gl + 1) * G],
                            in_=pb[6][:, dg * 256:(dg + 1) * 256]), inc=p_act)
                fence("act", ohh_v[0:1, 0:64].bitcast(F32)[0:1, 0:32], p_act)
                P.wait("sp", p_act)
                P.dma("sp", o_cont.rearrange("p (h t) -> p h t", h=H)[:, :, j * 512 + half * 256:j * 512 + half * 256 + 256],
                      ohh_v.rearrange("p (h t) -> p h t", h=H), s_os)
                if debug and j == 0 and half == 0:
                    P.wait("sp", p_dve)
                    P.dma("sp", dbg_gm, gm_sb[:], s_ld)
                P.wait("act", s_os)
            join(ENGS, [p_pe, p_act, p_dve, p_pool, s_os])

        for c in range(NCH):
            cast_stream(lambda pc, c=c: wq_sb[:, c * D + pc * 512:c * D + (pc + 1) * 512],
                        wo_d[c * 128:(c + 1) * 128, :], D)
        for c in range(NCH):
            dst = wkv_sb[:, c * D:(c + 1) * D] if c < 8 else wf2_sb[:, (c - 8) * D:(c - 7) * D]
            P.dma("sp", dst, wf_scr[c], s_kv)
            dma_flush(dst[:, 2046:2048].bitcast(F32), s_kv)
        join(ENGS, [p_dve, s_kv])

        for j in range(NT):
            tj = j * TILE
            P.wait("pool", p_pe)
            P.dma("pool", pkq_sb[:], o_cont[:, j * 8192:(j + 1) * 8192], s_os)
            dma_flush(pkq_sb[:, 8190:8192].bitcast(F32), s_os, eng="pool")
            pe0, pl0 = p_pe.n, p_pool.n
            for c in range(NCH):
                sl = c % 2
                slot = pbuf if sl == 0 else kbuf
                if c >= 2:
                    P.wait("sp", p_pe, pe0 + (c - 1))
                    P.wait("sp", p_pool, pl0 + 2 * (c - 1))
                P.dma("sp", slot, x2_d[c * 128:(c + 1) * 128, tj:tj + 512], s_x)
                dma_flush(slot[:, 511:512], s_x)
                P.wait("dve", s_x)
                P.op("dve", lambda e, slot=slot, sl=sl: e.tensor_copy(
                    out=xb16[:, sl * 512:(sl + 1) * 512], in_=slot))
                P.op("dve", lambda e, slot=slot, sl=sl: e.scalar_tensor_tensor(
                    out=qm_b[:, sl * 512:(sl + 1) * 512], in0=xb16[:, sl * 512:(sl + 1) * 512],
                    scalar=-1.0, in1=slot, op0=ALU.mult, op1=ALU.add), inc=p_dve)
                fence("dve", qm_b[0:1, sl * 512:sl * 512 + 64].bitcast(F32)[0:1, 0:32], p_dve)
                P.wait("pool", p_dve)
                P.op("pool", lambda e, sl=sl: e.tensor_mul(
                    out=qa_b[:, sl * 512:(sl + 1) * 512], in0=xb16[:, sl * 512:(sl + 1) * 512],
                    in1=xb16[:, sl * 512:(sl + 1) * 512]), inc=p_pool)
                fence("pool", qa_b[0:1, sl * 512:sl * 512 + 64].bitcast(F32)[0:1, 0:32], p_pool)
                P.wait("pe", p_pool)
                P.op("pe", lambda e, c=c, sl=sl: e.matmul(
                    out=pb[2][:], lhsT=o2_sb[:], rhs=xb16[:, sl * 512:(sl + 1) * 512],
                    start=(c == 0), stop=False))
                P.op("pe", lambda e, c=c, sl=sl: e.matmul(
                    out=pb[2][:], lhsT=o2_sb[:], rhs=qm_b[:, sl * 512:(sl + 1) * 512],
                    start=False, stop=(c == NCH - 1)))
                P.op("pe", lambda e, c=c, sl=sl: e.matmul(
                    out=pb[3][:], lhsT=o2_sb[:], rhs=qa_b[:, sl * 512:(sl + 1) * 512],
                    start=(c == 0), stop=(c == NCH - 1)), inc=p_pe)
            pe_flush()
            P.wait("dve", p_pe)
            P.op("dve", lambda e: e.tensor_scalar(out=lnM, in0=pb[2][:], scalar1=1.0 / D, scalar2=None, op0=ALU.mult))
            P.op("dve", lambda e: e.tensor_mul(out=lnT1, in0=lnM, in1=lnM))
            P.op("dve", lambda e: e.scalar_tensor_tensor(
                out=lnT2, in0=pb[3][:], scalar=1.0 / D, in1=lnT1, op0=ALU.mult, op1=ALU.subtract))
            fence("dve", lnT2[0:1, 0:64], p_dve)
            P.wait("act", p_dve)
            P.op("act", lambda e: e.activation(out=lnT1, in_=lnT2, func=AF.Sqrt, bias=ep_sb.ap()))
            fence("act", lnT1[0:1, 0:64], p_act)
            P.wait("dve", p_act)
            P.op("dve", lambda e: e.scalar_tensor_tensor(
                out=lnH, in0=lnT1, scalar=0.33, in1=lnM, op0=ALU.mult, op1=ALU.add))
            P.op("dve", lambda e: e.scalar_tensor_tensor(
                out=lnL, in0=lnT1, scalar=-0.33, in1=lnM, op0=ALU.mult, op1=ALU.add))
            fence("dve", lnL[0:1, 0:64], p_dve)
            join(("pool", "sp", "act", "pe"), [p_dve])
            for c in range(NCH):
                sl = c % 2
                slot = pbuf if sl == 0 else kbuf
                if c >= 1:
                    P.wait("act", p_dve)
                P.dma("act", slot, x2_d[c * 128:(c + 1) * 128, tj:tj + 512], s_x)
                dma_flush(slot[:, 511:512], s_x, eng="act")
                P.wait("dve", s_x)
                P.op("dve", lambda e, slot=slot, sl=sl: e.tensor_tensor(
                    out=qm_b[:, sl * 512:(sl + 1) * 512], in0=slot, in1=lnH, op=ALU.is_gt))
                P.op("dve", lambda e, slot=slot, sl=sl: e.tensor_tensor(
                    out=xb16[:, sl * 512:(sl + 1) * 512], in0=slot, in1=lnL, op=ALU.is_lt))
                P.op("dve", lambda e, c=c, sl=sl: e.scalar_tensor_tensor(
                    out=qp_sb[:, c * 512:(c + 1) * 512], in0=xb16[:, sl * 512:(sl + 1) * 512],
                    scalar=-1.0, in1=qm_b[:, sl * 512:(sl + 1) * 512], op0=ALU.mult, op1=ALU.add), inc=p_dve)
            fence("dve", qp_sb[0:1, 0:64].bitcast(F32)[0:1, 0:32], p_dve)
            join(("pe", "sp", "act", "pool"), [p_dve, s_os])
            if debug and j == 0:
                P.dma("sp", dbg_u2, qp_sb[:], s_ld)
                P.dma("sp", dbg_og, pkq_sb[:], s_ld)
                P.dma("sp", dbg_th[:, 0:512], lnH, s_ld)
                P.dma("sp", dbg_th[:, 512:1024], lnL, s_ld)
            dvB = p_dve.n
            acB = p_act.n
            for c in range(NCH):
                fb = pb[(0, 1, 4)[c % 3]]
                sb_ = pb[(2, 3, 5)[c % 3]]
                if c >= 2:
                    P.wait("pe", p_dve, dvB + 3 * (c - 1))
                    P.wait("pe", p_act, acB + 2 * (c - 1))
                for kc in range(NCH):
                    lhs = (wkv_sb[:, kc * D + c * 128:kc * D + (c + 1) * 128] if kc < 8
                           else wf2_sb[:, (kc - 8) * D + c * 128:(kc - 8) * D + (c + 1) * 128])
                    P.op("pe", lambda e, lhs=lhs, kc=kc, fb=fb: e.matmul(
                        out=fb[:], lhsT=lhs, rhs=qp_sb[:, kc * 512:(kc + 1) * 512],
                        start=(kc == 0), stop=(kc == NCH - 1)), inc=(p_pe if kc == NCH - 1 else None))
                pe_flush()
                P.wait("act", p_pe)
                P.wait("act", p_dve)
                P.op("act", lambda e, fb=fb: e.activation(out=lnM, in_=fb[:], func=AF.Silu), inc=p_act)
                fence("act", lnM[0:1, 0:64], p_act)
                for kc in range(NCH):
                    mov = pkq_sb.ap().rearrange("p (t s) -> p t s", s=G)[:, :, kc]
                    P.op("pe", lambda e, kc=kc, c=c, sb_=sb_, mov=mov: e.matmul(
                        out=sb_[:], lhsT=wq_sb[:, kc * D + c * 128:kc * D + (c + 1) * 128],
                        rhs=mov, start=(kc == 0), stop=(kc == NCH - 1)),
                        inc=(p_pe if kc == NCH - 1 else None))
                pe_flush()
                sl = c % 2
                slot = pbuf if sl == 0 else kbuf
                if c >= 2:
                    P.wait("sp", p_pool)
                P.dma("sp", slot, x2_d[c * 128:(c + 1) * 128, tj:tj + 512], s_x)
                dma_flush(slot[:, 511:512], s_x)
                P.wait("dve", p_pe)
                P.wait("dve", p_act)
                P.wait("dve", p_pool)
                P.op("dve", lambda e, c=c, sb_=sb_: e.tensor_scalar(
                    out=lnT1, in0=sb_[:], scalar1=bo_sb[:, c:c + 1], scalar2=None, op0=ALU.add))
                P.op("dve", lambda e: e.tensor_mul(out=lnT1, in0=lnT1, in1=lnM), inc=p_dve, inc_k=2)
                fence("dve", lnT1[0:1, 0:64], p_dve)
                if debug and j == 1:
                    P.wait("sp", p_dve)
                    P.dma("sp", dbg_g[:, c * 512:(c + 1) * 512], lnM, s_ld)
                    P.dma("sp", dbg_sg[:, c * 512:(c + 1) * 512], lnT1, s_ld)
                    join(("act", "dve", "pool"), [s_ld])
                P.wait("pool", p_dve)
                P.wait("pool", s_x)
                P.wait("pool", s_out)
                P.op("pool", lambda e, slot=slot: e.tensor_add(out=lnT2, in0=lnT1, in1=slot))
                fence("pool", lnT2[0:1, 0:64], p_pool)
                P.wait("sp", p_pool)
                P.dma("sp", out_d[c * 128:(c + 1) * 128, tj:tj + 512], lnT2, s_out)
                if PBS:
                    join(ENGS, [p_pe, p_act, p_dve, p_pool, s_x, s_out])
            join(ENGS, [p_pe, p_act, p_dve, p_pool])
            if debug and j == 2:
                P.dma("sp", dbg_u2b, qp_sb[:], s_ld)
                P.dma("sp", dbg_thb[:, 0:512], lnH, s_ld)
                P.dma("sp", dbg_thb[:, 512:1024], lnL, s_ld)
                P.dma("sp", dbg_og2, pkq_sb[:], s_ld)
                join(ENGS, [s_ld])

        P.wait("sp", s_out)
        P.wait("sp", s_os)
        P.wait("sp", s_ws)

        with nc.Block() as block:
            block.tensor(lambda e: [f(e) for f in P.streams["pe"]])
            block.scalar(lambda e: [f(e) for f in P.streams["act"]])
            block.vector(lambda e: [f(e) for f in P.streams["dve"]])
            block.gpsimd(lambda e: [f(e) for f in P.streams["pool"]])
            block.sync(lambda e: [f(e) for f in P.streams["sp"]])

    return nc


def _host_inputs(L0_attn_hidden_states, L0_attn_Wq, L0_attn_bq, L0_attn_Wk, L0_attn_bk,
                 L0_attn_Wv, L0_attn_bv, L0_attn_Wo, L0_attn_bo, L0_attn_rand_matrix,
                 L0_ffn_weight, L0_ffn_rram_variation):
    x = np.asarray(L0_attn_hidden_states, np.float32).reshape(2 * SEQ, D)
    x_fm = np.ascontiguousarray(x.T)
    wqT = np.ascontiguousarray(np.asarray(L0_attn_Wq, np.float32).T)
    wkT = np.ascontiguousarray(np.asarray(L0_attn_Wk, np.float32).T)
    wvT = np.ascontiguousarray(np.asarray(L0_attn_Wv, np.float32).T)
    woT = np.ascontiguousarray(np.asarray(L0_attn_Wo, np.float32).T)
    wfT = np.ascontiguousarray(np.asarray(L0_ffn_weight, np.float32).T)
    rrT = np.ascontiguousarray(np.asarray(L0_ffn_rram_variation, np.float32).T)
    rmT = np.ascontiguousarray(np.asarray(L0_attn_rand_matrix, np.float32).T)
    bqc = np.ascontiguousarray(np.asarray(L0_attn_bq, np.float32).reshape(16, 128).T)
    bkc = np.ascontiguousarray(np.asarray(L0_attn_bk, np.float32).reshape(8, 128).T)
    bvc = np.ascontiguousarray(np.asarray(L0_attn_bv, np.float32).reshape(8, 128).T)
    boc = np.ascontiguousarray(np.asarray(L0_attn_bo, np.float32).reshape(16, 128).T)

    inv_freq = 1.0 / THETA ** (np.arange(0, HD, 2, dtype=np.float64) / HD)
    ident = np.eye(128, dtype=BF)
    m1 = np.zeros((128, 256), dtype=np.float64)
    for hp in range(HKV):
        for t in range(G):
            for h in range(H):
                m1[hp * G + t, h * G + t] = 1.0 / 64.0
    mask2 = m1.astype(BF)
    oneh = np.zeros((128, 24 * 24), dtype=BF)
    for jj in range(24):
        oneh[:, jj * 24 + jj] = 1
    onesrow = np.ones((1, 128), dtype=BF)
    onescol = np.ones((128, 1), dtype=BF)
    ccol2 = np.full((128, 1), MAGIC, np.float32)
    ccol2[64:] = np.float32(MAGIC + 0.25)
    phcol = np.zeros((128, 1), np.float32)
    phcol[64:] = np.pi / 2
    epscol = np.full((128, 1), 1e-5, np.float32)

    hh_ = np.arange(H)
    in_maps = []
    for core in range(NCORE):
        b_ = core // 4
        i_ = core % 4
        sprime = (hh_[:, None] * 512 + i_ * 128 + np.arange(128)[None, :]).reshape(-1)
        x2c = np.ascontiguousarray(x_fm[:, b_ * SEQ + sprime])
        pos = (core % 4) * TCORE + np.arange(TCORE, dtype=np.float64)
        ang = pos[None, :] * inv_freq[:, None]
        cosT = np.concatenate([np.cos(ang), np.cos(ang)], axis=0).astype(BF)
        sinS = np.concatenate([-np.sin(ang), np.sin(ang)], axis=0).astype(BF)
        in_maps.append(dict(
            x=np.ascontiguousarray(x_fm[:, core * TCORE:(core + 1) * TCORE]),
            x2=x2c, ones2d=np.ones((128, 128), dtype=BF),
            wqT=wqT, wkT=wkT, wvT=wvT, woT=woT, wfT=wfT, rramT=rrT, rmT=rmT,
            bqc=bqc, bkc=bkc, bvc=bvc, boc=boc,
            cosT=cosT, sinS=sinS, ident=ident, mask2=mask2, oneh=oneh,
            onesrow=onesrow, onescol=onescol, ccol2=ccol2, phcol=phcol, epscol=epscol,
        ))
    return in_maps


def kernel(**inputs):
    in_maps = _host_inputs(**inputs)
    nc = build_program()
    res = run_bass_kernel_spmd(nc, in_maps, core_ids=list(range(NCORE)))
    full = np.empty((2, SEQ, D), np.float32)
    sp = np.arange(SEQ)
    i_of = (sp % 512) // 128
    tau_of = (sp // 512) * 128 + (sp % 128)
    for b in range(2):
        for i in range(4):
            o = np.asarray(res.results[b * 4 + i]["out"], np.float32)
            sel = sp[i_of == i]
            full[b, sel, :] = o[:, tau_of[sel]].T
    return full


# version 13
# speedup vs baseline: 1.0413x; 1.0413x over previous
import contextlib
import os as _os
import numpy as np
PBS = int(_os.environ.get("PBS", "0"))

import ml_dtypes
import concourse.bass as bass
import concourse.mybir as mybir
from concourse.bass_utils import run_bass_kernel_spmd

F32 = mybir.dt.float32
BF16 = mybir.dt.bfloat16
AF = mybir.ActivationFunctionType
ALU = mybir.AluOpType
BF = ml_dtypes.bfloat16

NCORE = 8
D = 2048
TCORE = 2048
TILE = 512
NT = TCORE // TILE
NCH = D // 128
H, HKV, HD, E = 16, 8, 128, 64
G = 16
NG = TILE // G
NGH = NG // 2
SEQ = 8192
THETA = 10000.0
MAGIC = float(1.5 * 2 ** 23)
INV2PI = float(1.0 / (2 * np.pi))
TWOPI = float(2 * np.pi)

ENGS = ("pe", "act", "dve", "pool", "sp")


class Sem:
    def __init__(self, es, nc, name):
        self.h = es.enter_context(nc.semaphore(name))
        self.n = 0


class Prog:

    def __init__(self):
        self.streams = {e: [] for e in ENGS}

    def op(self, eng, builder, inc=None, inc_k=1):
        if inc is not None:
            inc.n += inc_k

            def f(e, b=builder, h=inc.h, k=inc_k):
                b(e).then_inc(h, k)
        else:
            def f(e, b=builder):
                b(e)
        self.streams[eng].append(f)

    def wait(self, eng, sem, val=None):
        v = sem.n if val is None else val
        if v <= 0:
            return
        self.streams[eng].append(lambda e, h=sem.h, v=v: e.wait_ge(h, v))

    def dma(self, eng, out_ap, in_ap, sem):
        sem.n += 16
        self.streams[eng].append(
            lambda e, o=out_ap, i=in_ap, h=sem.h: e.dma_start(out=o, in_=i).then_inc(h, 16))


def build_program(debug=False):
    nc = bass.Bass("TRN2", target_bir_lowering=False)

    def inp(name, shape, dt=F32):
        return nc.dram_tensor(name, shape, dt, kind="ExternalInput").ap()

    x_d = inp("x", [D, TCORE])
    x2_d = inp("x2", [D, TCORE])
    wq_d = inp("wqT", [D, D])
    wk_d = inp("wkT", [D, HKV * HD])
    wv_d = inp("wvT", [D, HKV * HD])
    wo_d = inp("woT", [D, D])
    wf_d = inp("wfT", [D, D])
    rr_d = inp("rramT", [D, D])
    rm_d = inp("rmT", [128, E])
    bq_d = inp("bqc", [128, 16])
    bk_d = inp("bkc", [128, 8])
    bv_d = inp("bvc", [128, 8])
    bo_d = inp("boc", [128, 16])
    cos_d = inp("cosT", [128, TCORE], BF16)
    sin_d = inp("sinS", [128, TCORE], BF16)
    id_d = inp("ident", [128, 128], BF16)
    mk_d = inp("mask2", [128, 256], BF16)
    oh_d = inp("oneh", [128, 24 * 24], BF16)
    or_d = inp("onesrow", [1, 128], BF16)
    oc_d = inp("onescol", [128, 1], BF16)
    o2_d = inp("ones2d", [128, 128], BF16)
    cc_d = inp("ccol2", [128, 1])
    ph_d = inp("phcol", [128, 1])
    ep_d = inp("epscol", [128, 1])

    out_d = nc.dram_tensor("out", [D, TCORE], F32, kind="ExternalOutput").ap()
    if debug:
        dbg_xn = nc.dram_tensor("dbg_xn", [128, NCH * TILE], BF16, kind="ExternalOutput").ap()
        dbg_pkq = nc.dram_tensor("dbg_pkq", [128, H * TILE], BF16, kind="ExternalOutput").ap()
        dbg_pkk = nc.dram_tensor("dbg_pkk", [128, HKV * TILE], BF16, kind="ExternalOutput").ap()
        dbg_qp = nc.dram_tensor("dbg_qp", [128, H * TILE], BF16, kind="ExternalOutput").ap()
        dbg_kp = nc.dram_tensor("dbg_kp", [128, HKV * TILE], BF16, kind="ExternalOutput").ap()
        dbg_vf = nc.dram_tensor("dbg_vf", [128, HKV * TILE], BF16, kind="ExternalOutput").ap()
        dbg_inv = nc.dram_tensor("dbg_inv", [24, 512], F32, kind="ExternalOutput").ap()
        dbg_gm = nc.dram_tensor("dbg_gm", [128, 512], BF16, kind="ExternalOutput").ap()
        dbg_u2 = nc.dram_tensor("dbg_u2", [128, NCH * TILE], BF16, kind="ExternalOutput").ap()
        dbg_og = nc.dram_tensor("dbg_og", [128, H * TILE], BF16, kind="ExternalOutput").ap()
        dbg_th = nc.dram_tensor("dbg_th", [128, 1024], F32, kind="ExternalOutput").ap()
        dbg_u2b = nc.dram_tensor("dbg_u2b", [128, NCH * TILE], BF16, kind="ExternalOutput").ap()
        dbg_thb = nc.dram_tensor("dbg_thb", [128, 1024], F32, kind="ExternalOutput").ap()
        dbg_og2 = nc.dram_tensor("dbg_og2", [128, H * TILE], BF16, kind="ExternalOutput").ap()
        dbg_st = nc.dram_tensor("dbg_st", [128, 1024], F32, kind="ExternalOutput").ap()
        dbg_g = nc.dram_tensor("dbg_g", [128, NCH * TILE], F32, kind="ExternalOutput").ap()
        dbg_sg = nc.dram_tensor("dbg_sg", [128, NCH * TILE], F32, kind="ExternalOutput").ap()
        dbg_rows = nc.dram_tensor("dbg_rows", [1, 1024], F32, kind="ExternalOutput").ap()
        dbg_rows2 = nc.dram_tensor("dbg_rows2", [1, 1024], F32, kind="ExternalOutput").ap()

    scratch_kind = "ExternalOutput"
    o_cont = nc.dram_tensor("o_cont", [128, H * TCORE], BF16, kind=scratch_kind).ap()
    wk_scr = nc.dram_tensor("wk_scr", [NCH, 128, HKV * HD], BF16, kind="Internal").ap()
    wv_scr = nc.dram_tensor("wv_scr", [NCH, 128, HKV * HD], BF16, kind="Internal").ap()
    wf_scr = nc.dram_tensor("wf_scr", [NCH, 128, D], BF16, kind="Internal").ap()

    with contextlib.ExitStack() as es:
        A = es.enter_context

        wq_sb = A(nc.sbuf_tensor("wq_sb", [128, NCH * D], BF16))
        wkv_sb = A(nc.sbuf_tensor("wkv_sb", [128, NCH * 1024], BF16))
        wf2_sb = A(nc.sbuf_tensor("wf2_sb", [128, 8 * D], BF16))
        pkk_v = wf2_sb[:, 0:HKV * TILE]
        kp_v = wf2_sb[:, HKV * TILE:2 * HKV * TILE]
        vf_v = wf2_sb[:, 2 * HKV * TILE:3 * HKV * TILE]
        ohh_v = wf2_sb[:, 3 * HKV * TILE:4 * HKV * TILE]
        xn_sb = A(nc.sbuf_tensor("xn_sb", [128, NCH * TILE], BF16))
        pkq_sb = A(nc.sbuf_tensor("pkq_sb", [128, H * TILE], BF16))
        qp_sb = A(nc.sbuf_tensor("qp_sb", [128, H * TILE], BF16))
        scr = A(nc.sbuf_tensor("scr", [128, 4096], F32))
        lnM = scr[:, 0:512]
        lnT1 = scr[:, 512:1024]
        lnT2 = scr[:, 1024:1536]
        lnH = scr[:, 1536:2048]
        lnL = scr[:, 2048:2560]
        fsl = scr[:, 2560:3072]
        xsl = scr[:, 3072:4096]
        srow = lnH[0:1, :]
        sqrow = lnL[0:1, :]
        invrow = lnL[0:1, :]
        nsq_v = scr[0:24, 2560:3072]
        pbuf = scr[:, 3072:3584]
        kbuf = scr[:, 3584:4096]
        cosr = A(nc.sbuf_tensor("cosr", [128, 512], BF16))
        sinr = A(nc.sbuf_tensor("sinr", [128, 512], BF16))
        xb16 = A(nc.sbuf_tensor("xb16", [128, 1024], BF16))
        qa_b = A(nc.sbuf_tensor("qa_b", [128, 1024], BF16))
        qm_b = A(nc.sbuf_tensor("qm_b", [128, 1024], BF16))
        inv_sb = A(nc.sbuf_tensor("inv_sb", [24, 512], F32))
        gm_sb = A(nc.sbuf_tensor("gm_sb", [128, 512], BF16))
        v8_sb = A(nc.sbuf_tensor("v8_sb", [128, 256], BF16))
        rt_sb = A(nc.sbuf_tensor("rt_sb", [128, E], BF16))
        id_sb = A(nc.sbuf_tensor("id_sb", [128, 128], BF16))
        mk_sb = A(nc.sbuf_tensor("mk_sb", [128, 256], BF16))
        ohot = A(nc.sbuf_tensor("ohot", [128, 24 * 24], BF16))
        orow = A(nc.sbuf_tensor("orow", [1, 128], BF16))
        ocol = A(nc.sbuf_tensor("ocol", [128, 1], BF16))
        o2_sb = A(nc.sbuf_tensor("o2_sb", [128, 128], BF16))
        bq_sb = A(nc.sbuf_tensor("bq_sb", [128, 16], F32))
        bk_sb = A(nc.sbuf_tensor("bk_sb", [128, 8], F32))
        bv_sb = A(nc.sbuf_tensor("bv_sb", [128, 8], F32))
        bo_sb = A(nc.sbuf_tensor("bo_sb", [128, 16], F32))
        cc_sb = A(nc.sbuf_tensor("cc_sb", [128, 1], F32))
        ph_sb = A(nc.sbuf_tensor("ph_sb", [128, 1], F32))
        ep_sb = A(nc.sbuf_tensor("ep_sb", [128, 1], F32))
        flshb = A(nc.sbuf_tensor("flshb", [1, 128], BF16))
        flshf = A(nc.sbuf_tensor("flshf", [1, 64], F32))
        flshc = A(nc.sbuf_tensor("flshc", [128, 1], F32))


        pb = [A(nc.psum_tensor(f"pb{i}", [128, 512], F32)) for i in range(7)]
        pvt = A(nc.psum_tensor("pvt", [128, 512], BF16))
        pfl = pvt[0:1, 256:512].bitcast(F32)

        s_ld = Sem(es, nc, "s_ld")
        s_x = Sem(es, nc, "s_x")
        s_w = Sem(es, nc, "s_w")
        s_ws = Sem(es, nc, "s_ws")
        s_kv = Sem(es, nc, "s_kv")
        s_rot = Sem(es, nc, "s_rot")
        s_ir = Sem(es, nc, "s_ir")
        s_u = Sem(es, nc, "s_u")
        s_os = Sem(es, nc, "s_os")
        s_cs = Sem(es, nc, "s_cs")
        s_out = Sem(es, nc, "s_out")
        p_pe = Sem(es, nc, "p_pe")
        p_act = Sem(es, nc, "p_act")
        p_dve = Sem(es, nc, "p_dve")
        p_pool = Sem(es, nc, "p_pool")

        P = Prog()

        def fence(eng, src_ap, inc):
            n = src_ap.free_size()
            if eng == "act":
                P.op("act", lambda e, s=src_ap, n=n: e.copy(out=flshf[0:1, 0:n], in_=s), inc=inc)
            else:
                P.op(eng, lambda e, s=src_ap, n=n: e.tensor_copy(out=flshf[0:1, 0:n], in_=s), inc=inc)

        def dma_flush(dst_col_ap, sem, eng="sp"):
            P.dma(eng, flshc[:], dst_col_ap, sem)

        def pe_flush():
            P.op("pe", lambda e: e.matmul(out=pfl, lhsT=ocol[:], rhs=o2_sb[:],
                                          start=True, stop=True))
            P.op("pe", lambda e: e.matmul(out=pfl, lhsT=ocol[:], rhs=o2_sb[:],
                                          start=True, stop=True), inc=p_pe)

        def join(waiters=ENGS, producers=()):
            for w in waiters:
                for s in producers:
                    P.wait(w, s)

        for dst, src_ in [(id_sb[:], id_d), (mk_sb[:], mk_d),
                          (ohot[:], oh_d), (orow[:], or_d), (ocol[:], oc_d),
                          (bq_sb[:], bq_d), (bk_sb[:], bk_d), (bv_sb[:], bv_d),
                          (bo_sb[:], bo_d), (cc_sb[:], cc_d), (ph_sb[:], ph_d),
                          (ep_sb[:], ep_d), (o2_sb[:], o2_d)]:
            P.dma("sp", dst, src_, s_ld)
        P.dma("sp", pbuf[:, 0:E], rm_d, s_ld)
        P.wait("dve", s_ld)
        P.op("dve", lambda e: e.tensor_copy(out=rt_sb[:], in_=pbuf[:, 0:E]), inc=p_dve)

        def cast_stream(dst_of_piece, src_ap, ncols):
            for pc in range(ncols // 512):
                slot = pbuf if pc % 2 == 0 else kbuf
                P.wait("sp", p_dve)
                P.dma("sp", slot, src_ap[:, pc * 512:(pc + 1) * 512], s_w)
                P.wait("dve", s_w)
                P.op("dve", lambda e, slot=slot, pc=pc: e.tensor_copy(
                    out=dst_of_piece(pc), in_=slot), inc=p_dve)

        for c in range(NCH):
            cast_stream(lambda pc, c=c: wq_sb[:, c * D + pc * 512:c * D + (pc + 1) * 512],
                        wq_d[c * 128:(c + 1) * 128, :], D)
        for c in range(NCH):
            P.wait("dve", s_ws)
            cast_stream(lambda pc: qa_b[:, pc * 512:(pc + 1) * 512],
                        wk_d[c * 128:(c + 1) * 128, :], 1024)
            cast_stream(lambda pc: qm_b[:, pc * 512:(pc + 1) * 512],
                        wv_d[c * 128:(c + 1) * 128, :], 1024)
            P.wait("sp", p_dve)
            P.dma("sp", wk_scr[c], qa_b[:], s_ws)
            P.dma("sp", wv_scr[c], qm_b[:], s_ws)
        for c in range(NCH):
            for hh in range(2):
                P.wait("pool", s_ws)
                for pc in range(2):
                    col0 = hh * 1024 + pc * 512
                    P.wait("sp", p_pool)
                    P.wait("sp", p_dve)
                    P.dma("sp", pbuf, wf_d[c * 128:(c + 1) * 128, col0:col0 + 512], s_w)
                    P.dma("sp", kbuf, rr_d[c * 128:(c + 1) * 128, col0:col0 + 512], s_w)
                    P.wait("dve", s_w)
                    P.op("dve", lambda e: e.tensor_scalar(out=lnT2, in0=kbuf, scalar1=0.5, scalar2=None, op0=ALU.mult))
                    P.op("dve", lambda e: e.tensor_single_scalar(out=lnM, in_=pbuf, scalar=0.2, op=ALU.is_gt))
                    P.op("dve", lambda e: e.tensor_single_scalar(out=lnT1, in_=pbuf, scalar=-0.2, op=ALU.is_lt))
                    P.op("dve", lambda e: e.tensor_sub(out=lnM, in0=lnM, in1=lnT1))
                    P.op("dve", lambda e, pc=pc: e.scalar_tensor_tensor(
                        out=qa_b[:, pc * 512:(pc + 1) * 512], in0=lnM, scalar=0.3, in1=lnT2,
                        op0=ALU.mult, op1=ALU.add), inc=p_dve)
                P.wait("sp", p_dve)
                P.dma("sp", wf_scr[c][:, hh * 1024:(hh + 1) * 1024], qa_b[:], s_ws)
        join(ENGS, [p_dve, p_pool, s_ws, s_ld])

        for j in range(NT):
            t0 = j * TILE
            P.wait("sp", p_pool)
            P.dma("sp", cosr[:], cos_d[:, t0:t0 + 512], s_cs)
            P.dma("sp", sinr[:], sin_d[:, t0:t0 + 512], s_cs)
            P.dma("sp", flshc[:].bitcast(BF16)[:, 0:1], sinr[:, 511:512], s_cs)
            P.dma("sp", flshc[:].bitcast(BF16)[:, 0:1], cosr[:, 511:512], s_cs)
            P.wait("pool", p_pe)
            for c in range(NCH):
                P.dma("pool", wkv_sb[:, c * 1024:(c + 1) * 1024], wk_scr[c], s_kv)
                dma_flush(wkv_sb[:, c * 1024 + 1022:c * 1024 + 1024].bitcast(F32), s_kv, eng="pool")

            pe0, pl0 = p_pe.n, p_pool.n
            for c in range(NCH):
                sl = c % 2
                slot = pbuf if sl == 0 else kbuf
                if c >= 2:
                    P.wait("sp", p_pe, pe0 + (c - 1))
                    P.wait("sp", p_pool, pl0 + 2 * (c - 1))
                P.dma("sp", slot, x_d[c * 128:(c + 1) * 128, t0:t0 + 512], s_x)
                dma_flush(slot[:, 511:512], s_x)
                P.wait("dve", s_x)
                P.op("dve", lambda e, slot=slot, sl=sl: e.tensor_copy(
                    out=xb16[:, sl * 512:(sl + 1) * 512], in_=slot), inc=p_dve)
                fence("dve", xb16[0:1, sl * 512:sl * 512 + 64].bitcast(F32)[0:1, 0:32], p_dve)
                P.wait("pool", p_dve)
                P.op("pool", lambda e, sl=sl: e.tensor_mul(
                    out=qa_b[:, sl * 512:(sl + 1) * 512], in0=xb16[:, sl * 512:(sl + 1) * 512],
                    in1=xb16[:, sl * 512:(sl + 1) * 512]), inc=p_pool)
                fence("pool", qa_b[0:1, sl * 512:sl * 512 + 64].bitcast(F32)[0:1, 0:32], p_pool)
                P.wait("pe", p_pool)
                P.op("pe", lambda e, c=c, sl=sl: e.matmul(
                    out=pb[2][:], lhsT=o2_sb[:], rhs=xb16[:, sl * 512:(sl + 1) * 512],
                    start=(c == 0), stop=(c == NCH - 1)))
                P.op("pe", lambda e, c=c, sl=sl: e.matmul(
                    out=pb[3][:], lhsT=o2_sb[:], rhs=qa_b[:, sl * 512:(sl + 1) * 512],
                    start=(c == 0), stop=(c == NCH - 1)), inc=p_pe)
            pe_flush()
            P.wait("dve", p_pe)
            P.op("dve", lambda e: e.tensor_scalar(out=lnM, in0=pb[2][:], scalar1=1.0 / D, scalar2=None, op0=ALU.mult))
            P.op("dve", lambda e: e.tensor_mul(out=lnT1, in0=lnM, in1=lnM))
            P.op("dve", lambda e: e.scalar_tensor_tensor(
                out=lnT2, in0=pb[3][:], scalar=1.0 / D, in1=lnT1, op0=ALU.mult, op1=ALU.subtract))
            fence("dve", lnT2[0:1, 0:64], p_dve)
            P.wait("act", p_dve)
            P.op("act", lambda e: e.activation(out=lnT1, in_=lnT2, func=AF.Sqrt, bias=ep_sb.ap()))
            fence("act", lnT1[0:1, 0:64], p_act)
            P.wait("dve", p_act)
            P.op("dve", lambda e: e.reciprocal(out=lnT2, in_=lnT1))
            fence("dve", lnT2[0:1, 0:64], p_dve)
            join(("pool", "sp", "act"), [p_dve])
            if debug and j == 0:
                P.dma("sp", dbg_st[:, 0:512], lnM, s_ld)
                P.dma("sp", dbg_st[:, 512:1024], lnT2, s_ld)
                P.dma("sp", dbg_rows[0:1, 0:512], lnM[0:1, :], s_ld)
                P.dma("sp", dbg_rows[0:1, 512:1024], lnT2[0:1, :], s_ld)
                P.op("act", lambda e: e.copy(out=pbuf[0:1, :], in_=pb[0][0:1, :]))
                P.op("act", lambda e: e.copy(out=kbuf[0:1, :], in_=pb[1][0:1, :]))
                fence("act", kbuf[0:1, 0:64], p_act)
                P.wait("sp", p_act)
                P.dma("sp", dbg_rows2[0:1, 0:512], pbuf[0:1, :], s_ld)
                P.dma("sp", dbg_rows2[0:1, 512:1024], kbuf[0:1, :], s_ld)
                join(ENGS, [s_ld])

            for c in range(NCH):
                sl = c % 2
                slot = pbuf if sl == 0 else kbuf
                if c >= 1:
                    P.wait("sp", p_pool)
                P.dma("sp", slot, x_d[c * 128:(c + 1) * 128, t0:t0 + 512], s_x)
                dma_flush(slot[:, 511:512], s_x)
                P.wait("pool", s_x)
                P.op("pool", lambda e, slot=slot: e.tensor_sub(out=fsl, in0=slot, in1=lnM))
                P.op("pool", lambda e, c=c: e.tensor_mul(
                    out=xn_sb[:, c * 512:(c + 1) * 512], in0=fsl, in1=lnT2), inc=p_pool)
            join(ENGS, [p_pool])

            planes = [("q", i) for i in range(H)] + [("k", i) for i in range(HKV)] + [("v", i) for i in range(HKV)]
            evac_mark = []
            wv_loaded = False
            for idx, (kind, pi) in enumerate(planes):
                bank = pb[3 + (idx % 4)]
                if kind == "v" and not wv_loaded:
                    P.wait("sp", p_pe)
                    for c in range(NCH):
                        P.dma("sp", wkv_sb[:, c * 1024:(c + 1) * 1024], wv_scr[c], s_kv)
                        dma_flush(wkv_sb[:, c * 1024 + 1022:c * 1024 + 1024].bitcast(F32), s_kv)
                    wv_loaded = True
                if idx >= 4:
                    P.wait("pe", p_act, evac_mark[idx - 4])
                if kind == "q":
                    for c in range(NCH):
                        P.op("pe", lambda e, c=c, pi=pi, bank=bank: e.matmul(
                            out=bank[:], lhsT=wq_sb[:, c * D + pi * 128:c * D + (pi + 1) * 128],
                            rhs=xn_sb[:, c * 512:(c + 1) * 512], start=(c == 0), stop=(c == NCH - 1)),
                            inc=(p_pe if c == NCH - 1 else None))
                    pe_flush()
                else:
                    P.wait("pe", s_kv)
                    for c in range(NCH):
                        P.op("pe", lambda e, c=c, pi=pi, bank=bank: e.matmul(
                            out=bank[:], lhsT=wkv_sb[:, c * 1024 + pi * 128:c * 1024 + (pi + 1) * 128],
                            rhs=xn_sb[:, c * 512:(c + 1) * 512], start=(c == 0), stop=(c == NCH - 1)),
                            inc=(p_pe if c == NCH - 1 else None))
                    pe_flush()
                P.wait("act", p_pe)
                if kind == "v":
                    P.op("act", lambda e, pi=pi, bank=bank: e.activation(
                        out=vf_v.rearrange("p (g c) -> p g c", g=NG)[:, :, pi * G:(pi + 1) * G],
                        in_=bank[:], func=AF.Identity, bias=bv_sb[:, pi:pi + 1]), inc=p_act)
                    evac_mark.append(p_act.n)
                else:
                    half = pi % 2
                    bias = bq_sb[:, pi:pi + 1] if kind == "q" else bk_sb[:, pi:pi + 1]
                    if half == 0:
                        P.wait("act", p_pool)
                        P.wait("act", p_pe)
                    P.op("act", lambda e, half=half, bank=bank, bias=bias: e.activation(
                        out=qa_b[:, half * 512:(half + 1) * 512], in_=bank[:],
                        func=AF.Identity, bias=bias), inc=p_act)
                    evac_mark.append(p_act.n)
                    if half == 1:
                        park = pkq_sb[:] if kind == "q" else pkk_v
                        pcols = (pi - 1) * 512
                        P.wait("sp", p_act)
                        P.wait("sp", p_pool)
                        P.dma("sp", qm_b[0:64, :], qa_b[64:128, :], s_rot)
                        P.dma("sp", qm_b[64:128, :], qa_b[0:64, :], s_rot)
                        P.dma("sp", flshb[0:1, 0:64], qm_b[0:1, 960:1024], s_rot)
                        P.dma("sp", flshb[0:1, 64:128], qm_b[64:65, 960:1024], s_rot)
                        P.wait("pool", s_rot)
                        P.wait("pool", s_cs)
                        P.wait("pool", p_dve)
                        for hf in range(2):
                            P.op("pool", lambda e, hf=hf: e.tensor_mul(
                                out=xb16[:, hf * 512:(hf + 1) * 512],
                                in0=qa_b[:, hf * 512:(hf + 1) * 512], in1=cosr[:]))
                            P.op("pool", lambda e, hf=hf: e.tensor_mul(
                                out=qm_b[:, hf * 512:(hf + 1) * 512],
                                in0=qm_b[:, hf * 512:(hf + 1) * 512], in1=sinr[:]),
                                inc=(p_pool if hf == 1 else None))
                        P.wait("dve", p_pool)
                        P.op("dve", lambda e, park=park, pcols=pcols: e.tensor_add(
                            out=park[:, pcols:pcols + 1024], in0=xb16[:], in1=qm_b[:]), inc=p_dve)
                        fence("dve", park[0:1, pcols:pcols + 64].bitcast(F32)[0:1, 0:32], p_dve)
                        P.wait("act", p_dve)
                        P.op("act", lambda e, park=park, pcols=pcols: e.activation(
                            out=qa_b[:], in_=park[:, pcols:pcols + 1024], func=AF.Square), inc=p_act)
                        P.wait("pe", p_act)
                        jpl = (pi - 1) + (0 if kind == "q" else H)
                        for dpl in range(2):
                            jj = jpl + dpl
                            P.op("pe", lambda e, jj=jj, dpl=dpl: e.matmul(
                                out=pb[2][0:24, :], lhsT=ohot[:, jj * 24:(jj + 1) * 24],
                                rhs=qa_b[:, dpl * 512:(dpl + 1) * 512],
                                start=(jj == 0), stop=(jj == 23)), inc=p_pe)
            pe_flush()
            fence("act", vf_v[0:1, 0:64].bitcast(F32)[0:1, 0:32], p_act)
            join(ENGS, [p_pe, p_act, p_dve, p_pool])

            P.op("act", lambda e: e.copy(out=nsq_v, in_=pb[2][0:24, :]))
            fence("act", nsq_v[0:1, 0:64], p_act)
            P.wait("dve", p_act)
            P.op("dve", lambda e: e.tensor_scalar_max(out=nsq_v, in0=nsq_v, scalar1=1e-12))
            fence("dve", nsq_v[0:1, 0:64], p_dve)
            P.wait("act", p_dve)
            P.op("act", lambda e: e.activation(out=inv_sb[:], in_=nsq_v, func=AF.Sqrt))
            fence("act", inv_sb[0:1, 0:64], p_act)
            P.wait("dve", p_act)
            P.op("dve", lambda e: e.reciprocal(out=inv_sb[:], in_=inv_sb[:]))
            P.op("dve", lambda e: e.tensor_copy(out=qa_b[0:24, 0:512], in_=inv_sb[:]))
            P.op("dve", lambda e: e.tensor_sub(out=nsq_v, in0=inv_sb[:], in1=qa_b[0:24, 0:512]))
            P.op("dve", lambda e: e.tensor_copy(out=qa_b[0:24, 512:1024], in_=nsq_v))
            fence("dve", qa_b[0:24, 512:576].bitcast(F32)[0:1, 0:32], p_dve)
            join(("sp", "pe", "act", "pool"), [p_dve])

            if debug and j == 0:
                P.wait("sp", p_dve)
                P.dma("sp", dbg_xn, xn_sb[:], s_ld)
                P.dma("sp", dbg_pkq, pkq_sb[:], s_ld)
                P.dma("sp", dbg_pkk, pkk_v, s_ld)
                P.dma("sp", dbg_vf, vf_v, s_ld)
                P.dma("sp", dbg_inv, inv_sb[:], s_ld)

            irB, peB = s_ir.n, p_pe.n

            def s3c_row_dma(pl):
                rb = qm_b if pl % 2 == 0 else xb16
                P.dma("sp", rb[0:1, 0:512], qa_b[pl:pl + 1, 0:512], s_ir)
                P.dma("sp", rb[0:1, 512:1024], qa_b[pl:pl + 1, 512:1024], s_ir)
                P.dma("sp", flshb[0:1, 0:64], rb[0:1, 448:512], s_ir)
                P.dma("sp", flshb[0:1, 64:128], rb[0:1, 960:1024], s_ir)
            s3c_row_dma(0)
            for jpl in range(24):
                kind = "q" if jpl < H else "k"
                pi = jpl if jpl < H else jpl - H
                park = pkq_sb[:] if kind == "q" else pkk_v
                feats = qp_sb[:] if kind == "q" else kp_v
                rb = qm_b if jpl % 2 == 0 else xb16
                if jpl < 23:
                    if jpl >= 1:
                        P.wait("sp", p_pe, peB + 3 * jpl - 2)
                    s3c_row_dma(jpl + 1)
                P.wait("pe", s_ir, irB + 64 * (jpl + 1))
                P.wait("pe", p_act)
                P.wait("pe", p_dve)
                P.op("pe", lambda e, rb=rb: e.matmul(out=pb[5][:], lhsT=orow[:], rhs=rb[0:1, 0:512], start=True, stop=False))
                P.op("pe", lambda e, rb=rb: e.matmul(out=pb[5][:], lhsT=orow[:], rhs=rb[0:1, 512:1024], start=False, stop=True), inc=p_pe)
                P.op("pe", lambda e, park=park, pi=pi: e.matmul(
                    out=pb[6][0:64, :], lhsT=rt_sb[:], rhs=park[:, pi * 512:(pi + 1) * 512], start=True, stop=True))
                P.op("pe", lambda e, park=park, pi=pi: e.matmul(
                    out=pb[6][64:128, :], lhsT=rt_sb[:], rhs=park[:, pi * 512:(pi + 1) * 512], start=True, stop=True), inc=p_pe)
                pe_flush()
                P.wait("act", p_pe)
                P.wait("act", p_dve)
                P.op("act", lambda e: e.copy(out=fsl, in_=pb[5][:]), inc=p_act)
                P.wait("dve", p_pe)
                P.wait("dve", p_act)
                P.wait("dve", p_pool)
                P.op("dve", lambda e: e.tensor_mul(out=pbuf, in0=pb[6][:], in1=fsl))
                P.op("dve", lambda e: e.tensor_scalar(
                    out=kbuf, in0=pbuf, scalar1=INV2PI, scalar2=cc_sb[:, 0:1], op0=ALU.mult, op1=ALU.add))
                P.op("dve", lambda e: e.tensor_scalar(
                    out=kbuf, in0=kbuf, scalar1=MAGIC, scalar2=-TWOPI, op0=ALU.subtract, op1=ALU.mult), inc=p_dve)
                fence("dve", kbuf[0:1, 0:64], p_dve)
                P.wait("pool", p_dve)
                P.wait("pool", p_act)
                P.op("pool", lambda e: e.tensor_add(out=lnM, in0=kbuf, in1=pbuf), inc=p_pool)
                fence("pool", lnM[0:1, 0:64], p_pool)
                P.wait("act", p_pool)
                P.op("act", lambda e, feats=feats, pi=pi: e.activation(
                    out=feats.rearrange("p (g c) -> p g c", g=NG)[:, :, pi * G:(pi + 1) * G],
                    in_=lnM, func=AF.Sin, bias=ph_sb.ap()), inc=p_act)
            fence("act", qp_sb[0:1, 0:64].bitcast(F32)[0:1, 0:32], p_act)
            join(ENGS, [p_act, p_pe, p_dve, p_pool])

            if debug and j == 0:
                P.wait("sp", p_act)
                P.dma("sp", dbg_qp, qp_sb[:], s_ld)
                P.dma("sp", dbg_kp, kp_v, s_ld)

            for half in range(2):
                for gg in range(NGH // 2):
                    g0 = half * NGH + gg * 2
                    P.wait("pe", p_dve)
                    P.wait("pe", p_act)
                    for dg in range(2):
                        g = g0 + dg
                        P.op("pe", lambda e, g=g, dg=dg: e.matmul(
                            out=pb[5][:, dg * 256:(dg + 1) * 256], lhsT=kp_v[:, g * 128:(g + 1) * 128],
                            rhs=qp_sb[:, g * 256:(g + 1) * 256], start=True, stop=True))
                        P.op("pe", lambda e, g=g, dg=dg: e.transpose(
                            out=pvt[:, dg * 128:(dg + 1) * 128], in_=vf_v[:, g * 128:(g + 1) * 128],
                            identity=id_sb[:]), inc=(p_pe if dg == 1 else None))
                    pe_flush()
                    P.wait("dve", p_pe)
                    for dg in range(2):
                        P.op("dve", lambda e, dg=dg: e.tensor_mul(
                            out=gm_sb[:, dg * 256:(dg + 1) * 256],
                            in0=pb[5][:, dg * 256:(dg + 1) * 256], in1=mk_sb[:]),
                            inc=(p_dve if dg == 1 else None))
                    fence("dve", gm_sb[0:1, 0:64].bitcast(F32)[0:1, 0:32], p_dve)
                    P.wait("act", p_pe)
                    P.op("act", lambda e: e.copy(out=v8_sb[:], in_=pvt[:, 0:256]))
                    fence("act", v8_sb[0:1, 0:64].bitcast(F32)[0:1, 0:32], p_act)
                    P.wait("pe", p_dve)
                    P.wait("pe", p_act)
                    for dg in range(2):
                        P.op("pe", lambda e, dg=dg: e.matmul(
                            out=pb[6][:, dg * 256:(dg + 1) * 256], lhsT=v8_sb[:, dg * 128:(dg + 1) * 128],
                            rhs=gm_sb[:, dg * 256:(dg + 1) * 256], start=True, stop=True),
                            inc=(p_pe if dg == 1 else None))
                    pe_flush()
                    P.wait("act", p_pe)
                    for dg in range(2):
                        gl = (g0 + dg) % NGH
                        P.op("act", lambda e, dg=dg, gl=gl: e.copy(
                            out=ohh_v.rearrange("p (h t) -> p h t", h=H)[:, :, gl * G:(gl + 1) * G],
                            in_=pb[6][:, dg * 256:(dg + 1) * 256]), inc=p_act)
                fence("act", ohh_v[0:1, 0:64].bitcast(F32)[0:1, 0:32], p_act)
                P.wait("sp", p_act)
                P.dma("sp", o_cont.rearrange("p (h t) -> p h t", h=H)[:, :, j * 512 + half * 256:j * 512 + half * 256 + 256],
                      ohh_v.rearrange("p (h t) -> p h t", h=H), s_os)
                if debug and j == 0 and half == 0:
                    P.wait("sp", p_dve)
                    P.dma("sp", dbg_gm, gm_sb[:], s_ld)
                P.wait("act", s_os)
            join(ENGS, [p_pe, p_act, p_dve, p_pool, s_os])

        for c in range(NCH):
            cast_stream(lambda pc, c=c: wq_sb[:, c * D + pc * 512:c * D + (pc + 1) * 512],
                        wo_d[c * 128:(c + 1) * 128, :], D)
        for c in range(NCH):
            dst = wkv_sb[:, c * D:(c + 1) * D] if c < 8 else wf2_sb[:, (c - 8) * D:(c - 7) * D]
            P.dma("sp", dst, wf_scr[c], s_kv)
            dma_flush(dst[:, 2046:2048].bitcast(F32), s_kv)
        join(ENGS, [p_dve, s_kv])

        for j in range(NT):
            tj = j * TILE
            P.wait("pool", p_pe)
            P.dma("pool", pkq_sb[:], o_cont[:, j * 8192:(j + 1) * 8192], s_os)
            dma_flush(pkq_sb[:, 8190:8192].bitcast(F32), s_os, eng="pool")
            pe0, pl0 = p_pe.n, p_pool.n
            for c in range(NCH):
                sl = c % 2
                slot = pbuf if sl == 0 else kbuf
                if c >= 2:
                    P.wait("sp", p_pe, pe0 + (c - 1))
                    P.wait("sp", p_pool, pl0 + 2 * (c - 1))
                P.dma("sp", slot, x2_d[c * 128:(c + 1) * 128, tj:tj + 512], s_x)
                dma_flush(slot[:, 511:512], s_x)
                P.wait("dve", s_x)
                P.op("dve", lambda e, slot=slot, sl=sl: e.tensor_copy(
                    out=xb16[:, sl * 512:(sl + 1) * 512], in_=slot))
                P.op("dve", lambda e, slot=slot, sl=sl: e.scalar_tensor_tensor(
                    out=qm_b[:, sl * 512:(sl + 1) * 512], in0=xb16[:, sl * 512:(sl + 1) * 512],
                    scalar=-1.0, in1=slot, op0=ALU.mult, op1=ALU.add), inc=p_dve)
                fence("dve", qm_b[0:1, sl * 512:sl * 512 + 64].bitcast(F32)[0:1, 0:32], p_dve)
                P.wait("pool", p_dve)
                P.op("pool", lambda e, sl=sl: e.tensor_mul(
                    out=qa_b[:, sl * 512:(sl + 1) * 512], in0=xb16[:, sl * 512:(sl + 1) * 512],
                    in1=xb16[:, sl * 512:(sl + 1) * 512]), inc=p_pool)
                fence("pool", qa_b[0:1, sl * 512:sl * 512 + 64].bitcast(F32)[0:1, 0:32], p_pool)
                P.wait("pe", p_pool)
                P.op("pe", lambda e, c=c, sl=sl: e.matmul(
                    out=pb[2][:], lhsT=o2_sb[:], rhs=xb16[:, sl * 512:(sl + 1) * 512],
                    start=(c == 0), stop=False))
                P.op("pe", lambda e, c=c, sl=sl: e.matmul(
                    out=pb[2][:], lhsT=o2_sb[:], rhs=qm_b[:, sl * 512:(sl + 1) * 512],
                    start=False, stop=(c == NCH - 1)))
                P.op("pe", lambda e, c=c, sl=sl: e.matmul(
                    out=pb[3][:], lhsT=o2_sb[:], rhs=qa_b[:, sl * 512:(sl + 1) * 512],
                    start=(c == 0), stop=(c == NCH - 1)), inc=p_pe)
            pe_flush()
            P.wait("dve", p_pe)
            P.op("dve", lambda e: e.tensor_scalar(out=lnM, in0=pb[2][:], scalar1=1.0 / D, scalar2=None, op0=ALU.mult))
            P.op("dve", lambda e: e.tensor_mul(out=lnT1, in0=lnM, in1=lnM))
            P.op("dve", lambda e: e.scalar_tensor_tensor(
                out=lnT2, in0=pb[3][:], scalar=1.0 / D, in1=lnT1, op0=ALU.mult, op1=ALU.subtract))
            fence("dve", lnT2[0:1, 0:64], p_dve)
            P.wait("act", p_dve)
            P.op("act", lambda e: e.activation(out=lnT1, in_=lnT2, func=AF.Sqrt, bias=ep_sb.ap()))
            fence("act", lnT1[0:1, 0:64], p_act)
            P.wait("dve", p_act)
            P.op("dve", lambda e: e.scalar_tensor_tensor(
                out=lnH, in0=lnT1, scalar=0.33, in1=lnM, op0=ALU.mult, op1=ALU.add))
            P.op("dve", lambda e: e.scalar_tensor_tensor(
                out=lnL, in0=lnT1, scalar=-0.33, in1=lnM, op0=ALU.mult, op1=ALU.add))
            fence("dve", lnL[0:1, 0:64], p_dve)
            join(("pool", "sp", "act", "pe"), [p_dve])
            for c in range(NCH):
                sl = c % 2
                slot = pbuf if sl == 0 else kbuf
                if c >= 1:
                    P.wait("sp", p_dve)
                P.dma("sp", slot, x2_d[c * 128:(c + 1) * 128, tj:tj + 512], s_x)
                dma_flush(slot[:, 511:512], s_x)
                P.wait("dve", s_x)
                P.op("dve", lambda e, slot=slot, sl=sl: e.tensor_tensor(
                    out=qm_b[:, sl * 512:(sl + 1) * 512], in0=slot, in1=lnH, op=ALU.is_gt))
                P.op("dve", lambda e, slot=slot, sl=sl: e.tensor_tensor(
                    out=xb16[:, sl * 512:(sl + 1) * 512], in0=slot, in1=lnL, op=ALU.is_lt))
                P.op("dve", lambda e, c=c, sl=sl: e.scalar_tensor_tensor(
                    out=qp_sb[:, c * 512:(c + 1) * 512], in0=xb16[:, sl * 512:(sl + 1) * 512],
                    scalar=-1.0, in1=qm_b[:, sl * 512:(sl + 1) * 512], op0=ALU.mult, op1=ALU.add), inc=p_dve)
            fence("dve", qp_sb[0:1, 0:64].bitcast(F32)[0:1, 0:32], p_dve)
            join(("pe", "sp", "act", "pool"), [p_dve, s_os])
            if debug and j == 0:
                P.dma("sp", dbg_u2, qp_sb[:], s_ld)
                P.dma("sp", dbg_og, pkq_sb[:], s_ld)
                P.dma("sp", dbg_th[:, 0:512], lnH, s_ld)
                P.dma("sp", dbg_th[:, 512:1024], lnL, s_ld)
            dvB = p_dve.n
            acB = p_act.n
            for c in range(NCH):
                fb = pb[(0, 1, 4)[c % 3]]
                sb_ = pb[(2, 3, 5)[c % 3]]
                if c >= 2:
                    P.wait("pe", p_dve, dvB + 3 * (c - 1))
                    P.wait("pe", p_act, acB + 2 * (c - 1))
                for kc in range(NCH):
                    lhs = (wkv_sb[:, kc * D + c * 128:kc * D + (c + 1) * 128] if kc < 8
                           else wf2_sb[:, (kc - 8) * D + c * 128:(kc - 8) * D + (c + 1) * 128])
                    P.op("pe", lambda e, lhs=lhs, kc=kc, fb=fb: e.matmul(
                        out=fb[:], lhsT=lhs, rhs=qp_sb[:, kc * 512:(kc + 1) * 512],
                        start=(kc == 0), stop=(kc == NCH - 1)), inc=(p_pe if kc == NCH - 1 else None))
                pe_flush()
                P.wait("act", p_pe)
                P.wait("act", p_dve)
                P.op("act", lambda e, fb=fb: e.activation(out=lnM, in_=fb[:], func=AF.Silu), inc=p_act)
                fence("act", lnM[0:1, 0:64], p_act)
                for kc in range(NCH):
                    mov = pkq_sb.ap().rearrange("p (t s) -> p t s", s=G)[:, :, kc]
                    P.op("pe", lambda e, kc=kc, c=c, sb_=sb_, mov=mov: e.matmul(
                        out=sb_[:], lhsT=wq_sb[:, kc * D + c * 128:kc * D + (c + 1) * 128],
                        rhs=mov, start=(kc == 0), stop=(kc == NCH - 1)),
                        inc=(p_pe if kc == NCH - 1 else None))
                pe_flush()
                sl = c % 2
                slot = pbuf if sl == 0 else kbuf
                if c >= 2:
                    P.wait("sp", p_pool)
                P.dma("sp", slot, x2_d[c * 128:(c + 1) * 128, tj:tj + 512], s_x)
                dma_flush(slot[:, 511:512], s_x)
                P.wait("dve", p_pe)
                P.wait("dve", p_act)
                P.wait("dve", p_pool)
                P.op("dve", lambda e, c=c, sb_=sb_: e.tensor_scalar(
                    out=lnT1, in0=sb_[:], scalar1=bo_sb[:, c:c + 1], scalar2=None, op0=ALU.add))
                P.op("dve", lambda e: e.tensor_mul(out=lnT1, in0=lnT1, in1=lnM), inc=p_dve, inc_k=2)
                fence("dve", lnT1[0:1, 0:64], p_dve)
                if debug and j == 1:
                    P.wait("sp", p_dve)
                    P.dma("sp", dbg_g[:, c * 512:(c + 1) * 512], lnM, s_ld)
                    P.dma("sp", dbg_sg[:, c * 512:(c + 1) * 512], lnT1, s_ld)
                    join(("act", "dve", "pool"), [s_ld])
                P.wait("pool", p_dve)
                P.wait("pool", s_x)
                P.wait("pool", s_out)
                P.op("pool", lambda e, slot=slot: e.tensor_add(out=lnT2, in0=lnT1, in1=slot))
                fence("pool", lnT2[0:1, 0:64], p_pool)
                P.wait("sp", p_pool)
                P.dma("sp", out_d[c * 128:(c + 1) * 128, tj:tj + 512], lnT2, s_out)
                if PBS:
                    join(ENGS, [p_pe, p_act, p_dve, p_pool, s_x, s_out])
            join(ENGS, [p_pe, p_act, p_dve, p_pool])
            if debug and j == 2:
                P.dma("sp", dbg_u2b, qp_sb[:], s_ld)
                P.dma("sp", dbg_thb[:, 0:512], lnH, s_ld)
                P.dma("sp", dbg_thb[:, 512:1024], lnL, s_ld)
                P.dma("sp", dbg_og2, pkq_sb[:], s_ld)
                join(ENGS, [s_ld])

        P.wait("sp", s_out)
        P.wait("sp", s_os)
        P.wait("sp", s_ws)

        with nc.Block() as block:
            block.tensor(lambda e: [f(e) for f in P.streams["pe"]])
            block.scalar(lambda e: [f(e) for f in P.streams["act"]])
            block.vector(lambda e: [f(e) for f in P.streams["dve"]])
            block.gpsimd(lambda e: [f(e) for f in P.streams["pool"]])
            block.sync(lambda e: [f(e) for f in P.streams["sp"]])

    return nc


def _host_inputs(L0_attn_hidden_states, L0_attn_Wq, L0_attn_bq, L0_attn_Wk, L0_attn_bk,
                 L0_attn_Wv, L0_attn_bv, L0_attn_Wo, L0_attn_bo, L0_attn_rand_matrix,
                 L0_ffn_weight, L0_ffn_rram_variation):
    x = np.asarray(L0_attn_hidden_states, np.float32).reshape(2 * SEQ, D)
    x_fm = np.ascontiguousarray(x.T)
    wqT = np.ascontiguousarray(np.asarray(L0_attn_Wq, np.float32).T)
    wkT = np.ascontiguousarray(np.asarray(L0_attn_Wk, np.float32).T)
    wvT = np.ascontiguousarray(np.asarray(L0_attn_Wv, np.float32).T)
    woT = np.ascontiguousarray(np.asarray(L0_attn_Wo, np.float32).T)
    wfT = np.ascontiguousarray(np.asarray(L0_ffn_weight, np.float32).T)
    rrT = np.ascontiguousarray(np.asarray(L0_ffn_rram_variation, np.float32).T)
    rmT = np.ascontiguousarray(np.asarray(L0_attn_rand_matrix, np.float32).T)
    bqc = np.ascontiguousarray(np.asarray(L0_attn_bq, np.float32).reshape(16, 128).T)
    bkc = np.ascontiguousarray(np.asarray(L0_attn_bk, np.float32).reshape(8, 128).T)
    bvc = np.ascontiguousarray(np.asarray(L0_attn_bv, np.float32).reshape(8, 128).T)
    boc = np.ascontiguousarray(np.asarray(L0_attn_bo, np.float32).reshape(16, 128).T)

    inv_freq = 1.0 / THETA ** (np.arange(0, HD, 2, dtype=np.float64) / HD)
    ident = np.eye(128, dtype=BF)
    m1 = np.zeros((128, 256), dtype=np.float64)
    for hp in range(HKV):
        for t in range(G):
            for h in range(H):
                m1[hp * G + t, h * G + t] = 1.0 / 64.0
    mask2 = m1.astype(BF)
    oneh = np.zeros((128, 24 * 24), dtype=BF)
    for jj in range(24):
        oneh[:, jj * 24 + jj] = 1
    onesrow = np.ones((1, 128), dtype=BF)
    onescol = np.ones((128, 1), dtype=BF)
    ccol2 = np.full((128, 1), MAGIC, np.float32)
    ccol2[64:] = np.float32(MAGIC + 0.25)
    phcol = np.zeros((128, 1), np.float32)
    phcol[64:] = np.pi / 2
    epscol = np.full((128, 1), 1e-5, np.float32)

    hh_ = np.arange(H)
    in_maps = []
    for core in range(NCORE):
        b_ = core // 4
        i_ = core % 4
        sprime = (hh_[:, None] * 512 + i_ * 128 + np.arange(128)[None, :]).reshape(-1)
        x2c = np.ascontiguousarray(x_fm[:, b_ * SEQ + sprime])
        pos = (core % 4) * TCORE + np.arange(TCORE, dtype=np.float64)
        ang = pos[None, :] * inv_freq[:, None]
        cosT = np.concatenate([np.cos(ang), np.cos(ang)], axis=0).astype(BF)
        sinS = np.concatenate([-np.sin(ang), np.sin(ang)], axis=0).astype(BF)
        in_maps.append(dict(
            x=np.ascontiguousarray(x_fm[:, core * TCORE:(core + 1) * TCORE]),
            x2=x2c, ones2d=np.ones((128, 128), dtype=BF),
            wqT=wqT, wkT=wkT, wvT=wvT, woT=woT, wfT=wfT, rramT=rrT, rmT=rmT,
            bqc=bqc, bkc=bkc, bvc=bvc, boc=boc,
            cosT=cosT, sinS=sinS, ident=ident, mask2=mask2, oneh=oneh,
            onesrow=onesrow, onescol=onescol, ccol2=ccol2, phcol=phcol, epscol=epscol,
        ))
    return in_maps


def kernel(**inputs):
    in_maps = _host_inputs(**inputs)
    nc = build_program()
    res = run_bass_kernel_spmd(nc, in_maps, core_ids=list(range(NCORE)))
    full = np.empty((2, SEQ, D), np.float32)
    sp = np.arange(SEQ)
    i_of = (sp % 512) // 128
    tau_of = (sp // 512) * 128 + (sp % 128)
    for b in range(2):
        for i in range(4):
            o = np.asarray(res.results[b * 4 + i]["out"], np.float32)
            sel = sp[i_of == i]
            full[b, sel, :] = o[:, tau_of[sel]].T
    return full
